# version 14
# baseline (speedup 1.0000x reference)
import contextlib
import math

import numpy as np
import ml_dtypes

import concourse.bass as bass
import concourse.mybir as mybir
from concourse.bass_utils import run_bass_kernel_spmd

F32 = mybir.dt.float32
BF16 = mybir.dt.bfloat16
I32 = mybir.dt.int32
AF = mybir.ActivationFunctionType
ALU = mybir.AluOpType

D = 4096
T = 1024
NKV = 8
G = 4
HD = 128
SM = 1.0 / math.sqrt(HD)
EPS = 1e-5
ESH = -20.0
NEGB = -1.0e9
BF = ml_dtypes.bfloat16


class Ctx:
    def __init__(self, nc):
        self.nc = nc
        self.c = {}
        self.sem = {}

    def mk(self, es, names):
        for n in names:
            self.sem[n] = es.enter_context(self.nc.semaphore(n))
            self.c[n] = 0

    def inc(self, inst, name, n=1):
        inst.then_inc(self.sem[name], n)
        self.c[name] += n
        return (name, self.c[name])

    def dinc(self, inst, name):
        return self.inc(inst, name, 16)

    def mark(self, name):
        return (name, self.c[name])

    def wait(self, eng, mark):
        name, val = mark
        assert val <= self.c[name], (name, val, self.c[name])
        if val > 0:
            eng.wait_ge(self.sem[name], val)


Z = ("pe", 0)


def build(taps=(), paranoid=0):
    nc = bass.Bass("TRN2", target_bir_lowering=False)

    xo = nc.dram_tensor("xo", [T, D], F32, kind="ExternalInput").ap()
    xh = nc.dram_tensor("xh", [T, D], F32, kind="ExternalInput").ap()
    wq = nc.dram_tensor("wq", [D, D], F32, kind="ExternalInput").ap()
    wk = nc.dram_tensor("wk", [D, NKV * HD], F32, kind="ExternalInput").ap()
    wv = nc.dram_tensor("wv", [D, NKV * HD], F32, kind="ExternalInput").ap()
    wo = nc.dram_tensor("wo", [D, D], F32, kind="ExternalInput").ap()
    qs = nc.dram_tensor("qs", [D, 128], F32, kind="ExternalInput").ap()
    qvals = nc.dram_tensor("qvals", [D, D], I32, kind="ExternalInput").ap()
    cs1o = nc.dram_tensor("cs1o", [128, T], F32, kind="ExternalInput").ap()
    cs2o = nc.dram_tensor("cs2o", [128, T], F32, kind="ExternalInput").ap()
    cs1h = nc.dram_tensor("cs1h", [128, T], F32, kind="ExternalInput").ap()
    cs2h = nc.dram_tensor("cs2h", [128, T], F32, kind="ExternalInput").ap()
    m0i = nc.dram_tensor("m0i", [128, 128], BF16, kind="ExternalInput").ap()
    m8i = nc.dram_tensor("m8i", [128, 128], BF16, kind="ExternalInput").ap()
    sbi = nc.dram_tensor("sbi", [128, 16], F32, kind="ExternalInput").ap()
    onei = nc.dram_tensor("onei", [128, 1], BF16, kind="ExternalInput").ap()
    onk1 = nc.dram_tensor("onk1", [1, 128], BF16, kind="ExternalInput").ap()
    idti = nc.dram_tensor("idti", [128, 128], BF16, kind="ExternalInput").ap()
    bri = nc.dram_tensor("bri", [128, 32], F32, kind="ExternalInput").ap()
    epsi = nc.dram_tensor("epsi", [128, 1], F32, kind="ExternalInput").ap()
    prmi = nc.dram_tensor("prmi", [128, 128], BF16, kind="ExternalInput").ap()

    y = nc.dram_tensor("y", [T, D], F32, kind="ExternalOutput").ap()

    d_h = [nc.dram_tensor(f"d_h{i}", [T, D], BF16, kind="Internal").ap() for i in range(2)]
    d_h2 = nc.dram_tensor("d_h2", [T, D], BF16, kind="Internal").ap()
    d_K = nc.dram_tensor("d_K", [NKV, HD, 2 * T], BF16, kind="Internal").ap()
    d_V = nc.dram_tensor("d_V", [NKV, 16, 128, HD], BF16, kind="Internal").ap()
    d_x1 = nc.dram_tensor("d_x1", [T, D], F32, kind="Internal").ap()
    d_ar = nc.dram_tensor("d_ar", [D, T], BF16, kind="Internal").ap()
    d_fr = nc.dram_tensor("d_fr", [D, T], BF16, kind="Internal").ap()
    d_W = nc.dram_tensor("d_W", [D, D], BF16, kind="Internal").ap()
    d_Wt = nc.dram_tensor("d_Wt", [D, D], BF16, kind="Internal").ap()
    d_fn = nc.dram_tensor("d_fn", [128, 64], BF16, kind="Internal").ap()

    tapspec = {"d_h0": (d_h[0], [T, D], BF16), "d_h1": (d_h[1], [T, D], BF16),
               "d_h2": (d_h2, [T, D], BF16),
               "d_K": (d_K, [NKV, HD, 2 * T], BF16),
               "d_V": (d_V, [NKV, 16, 128, HD], BF16),
               "d_x1": (d_x1, [T, D], F32), "d_ar": (d_ar, [D, T], BF16),
               "d_fr": (d_fr, [D, T], BF16), "d_W": (d_W, [D, D], BF16),
               "d_Wt": (d_Wt, [D, D], BF16)}
    tap_out = {}
    for tname in taps:
        src, shp, dt = tapspec[tname]
        tap_out[tname] = (nc.dram_tensor("tap_" + tname, shp, dt,
                                         kind="ExternalOutput").ap(), src)

    ctx = Ctx(nc)

    with contextlib.ExitStack() as es:
        def sb(name, shape, dt):
            return es.enter_context(nc.sbuf_tensor(name, shape, dt))

        c_cs1 = sb("c_cs1", [128, T], F32)
        c_cs2 = sb("c_cs2", [128, T], F32)
        c_m0 = sb("c_m0", [128, 128], BF16)
        c_m8 = sb("c_m8", [128, 128], BF16)
        c_sb = sb("c_sb", [128, 16], F32)
        c_one = sb("c_one", [128, 1], BF16)
        c_ok1 = sb("c_ok1", [1, 128], BF16)
        c_idt = sb("c_idt", [128, 128], BF16)
        c_br = sb("c_br", [128, 32], F32)
        c_eps = sb("c_eps", [128, 1], F32)
        c_prm = sb("c_prm", [128, 128], BF16)

        A = sb("A", [128, 32 * T], BF16)
        QG = [sb(f"qg{h}", [128, G * T], BF16) for h in range(NKV)]
        s_w = sb("s_w", [128, 4 * T], BF16)
        s_p2 = sb("s_p2", [128, 6 * T], BF16)
        s_hx = sb("s_hx", [128, 8 * T], BF16)
        s_xq = sb("s_xq", [128, 2 * T], F32)
        s_y = sb("s_y", [128, 2 * T], F32)
        s_bcb = sb("s_bcb", [128, 512], F32)
        s_dsf = sb("s_dsf", [1, 512], F32)
        s_dsb = sb("s_dsb", [1, 1024], BF16)
        s_bn6 = sb("s_bn6", [128, 48], F32)
        s_bn2 = sb("s_bn2", [128, 2], F32)
        s_lnv = sb("s_lnv", [128, 1], F32)
        s_rst = sb("s_rst", [128, 1], F32)
        s_mb = sb("s_mb", [128, 1], F32)
        s_fl = sb("s_fl", [1, 8], F32)
        s_wf = sb("s_wf", [128, 2 * T], F32)

        p_all = es.enter_context(nc.psum_tensor("p_all", [128, 8 * 512], F32))

        def bank(i, n=1):
            return p_all[:, i * 512:(i + n) * 512]

        ctx.mk(es, ["dC", "dW", "dX", "dHW", "dH", "dKV", "dAR", "dAT",
                    "dX1", "dY", "dQV", "dWW", "dWT", "dWX", "dTAP",
                    "pe", "vv", "ss", "pp"])
        for sm in ctx.sem.values():
            nc.gpsimd.sem_clear(sm)
        nc.all_engine_barrier()

        block = es.enter_context(nc.Block())

        def on(name):
            return getattr(block, name)

        def vflush(v, name):
            inst = v.tensor_copy(s_fl[:, 4:8], s_fl[:, 0:4])
            return ctx.inc(inst, name)

        def sflush(s_, name):
            inst = s_.copy(s_fl[:, 4:8], s_fl[:, 0:4])
            return ctx.inc(inst, name)

        def phase_barrier(level=1):
            if paranoid < level:
                return
            marks = [ctx.mark(n) for n in
                     ("pe", "vv", "ss", "dW", "dX", "dHW", "dH", "dKV", "dAR",
                      "dAT", "dX1", "dY", "dQV", "dWW", "dWT", "dWX")]
            for eng in ("tensor", "vector", "scalar", "sync", "gpsimd"):
                @on(eng)
                def _(e):
                    for m in marks:
                        ctx.wait(e, m)

        def wslot(i):
            return s_w[:, i * T:(i + 1) * T]

        wst = {"free": [Z, Z, Z, Z], "ld": [Z, Z, Z, Z], "rr": 0,
               "f32free": [Z, Z]}

        def wf(j):
            return s_wf[:, j * T:(j + 1) * T]

        def emit_wload(queue_name, src_col_qtr, cast):
            i = wst["rr"] % 4
            j = wst["rr"] % 2
            wst["rr"] += 1
            gate = wst["free"][i]
            if not cast:
                @on(queue_name)
                def _(g):
                    ctx.wait(g, gate)
                    inst = g.dma_start(
                        out=wslot(i).rearrange("p (f c) -> p f c", f=8),
                        in_=src_col_qtr.rearrange("(f p) c -> p f c", p=128))
                    wst["ld"][i] = ctx.dinc(inst, "dW")
                return i
            f32gate = wst["f32free"][j]

            @on("sync")
            def _(s):
                ctx.wait(s, f32gate)
                inst = s.dma_start(
                    out=wf(j).rearrange("p (f c) -> p f c", f=8),
                    in_=src_col_qtr.rearrange("(f p) c -> p f c", p=128))
                ctx.dinc(inst, "dW")
            ldm = ctx.mark("dW")

            @on("scalar")
            def _(s_):
                ctx.wait(s_, ldm)
                ctx.wait(s_, gate)
                s_.activation(wslot(i), wf(j), AF.Identity, scale=1.0)
                wst["ld"][i] = sflush(s_, "ss")
                wst["f32free"][j] = wst["ld"][i]
            return i

        def emit_proj_chunk(w_col_ap, rhs_fn, pb, pair_gate, cast=True,
                            queue_name="gpsimd"):
            slots = [emit_wload(queue_name, w_col_ap[q * 1024:(q + 1) * 1024, :], cast)
                     for q in range(4)]

            @on("tensor")
            def _(t):
                ctx.wait(t, pair_gate)
                for qtr in range(4):
                    i = slots[qtr]
                    ctx.wait(t, wst["ld"][i])
                    for f8 in range(8):
                        ft = qtr * 8 + f8
                        lhsT = wslot(i)[:, f8 * 128:(f8 + 1) * 128]
                        first = (ft == 0)
                        last = (ft == 31)
                        rhs = rhs_fn(ft)
                        t.matmul(bank(pb), lhsT, rhs[:, 0:512],
                                 start=first, stop=last)
                        mm = t.matmul(bank(pb + 1), lhsT, rhs[:, 512:1024],
                                      start=first, stop=last)
                        if f8 == 7:
                            wst["free"][i] = ctx.inc(mm, "pe")
            return ctx.mark("pe")

        kbf = s_p2[:, 0:1024]
        tf1 = s_p2[:, 1024:3072].bitcast(F32)
        tf2 = s_p2[:, 3072:5120].bitcast(F32)
        rp = {"swap_pe": Z, "rope_vv": Z}

        def emit_rope_evac(pb, dest_ap, pe_stop, extra_gate=Z, bank_gate=Z):
            @on("vector")
            def _(v):
                ctx.wait(v, pe_stop)
                ctx.wait(v, extra_gate)
                ctx.wait(v, rp["swap_pe"])
                v.tensor_copy(kbf, bank(pb, 2))
                vflush(v, "vv")
            kbf_vv = ctx.mark("vv")

            @on("tensor")
            def _(t):
                ctx.wait(t, kbf_vv)
                ctx.wait(t, rp["rope_vv"])
                ctx.wait(t, bank_gate)
                t.matmul(bank(4), c_prm[:], kbf[:, 0:512], start=True, stop=True)
                inst = t.matmul(bank(5), c_prm[:], kbf[:, 512:1024],
                                start=True, stop=True)
                rp["swap_pe"] = ctx.inc(inst, "pe")

            @on("vector")
            def _(v):
                ctx.wait(v, rp["swap_pe"])
                v.tensor_tensor(tf1, bank(pb, 2), c_cs1[:], ALU.mult)
                v.tensor_tensor(tf2, bank(4, 2), c_cs2[:], ALU.mult)
                v.tensor_tensor(dest_ap, tf1, tf2, ALU.add)
                rp["rope_vv"] = vflush(v, "vv")
            return ctx.mark("vv")

        @on("sync")
        def _(s):
            for dst, src in [(c_cs1, cs1h), (c_cs2, cs2h), (c_m0, m0i),
                             (c_m8, m8i), (c_sb, sbi), (c_one, onei),
                             (c_idt, idti), (c_br, bri), (c_eps, epsi),
                             (c_prm, prmi)]:
                ctx.dinc(s.dma_start(out=dst[:], in_=src), "dC")

        @on("gpsimd")
        def _(g):
            ctx.dinc(g.dma_start(out=c_ok1[:], in_=onk1), "dC")
        consts_done = ctx.mark("dC")

        def hx(i):
            return s_hx[:, i * 2048:(i + 1) * 2048]

        xt = s_hx[:].bitcast(F32)
        hout = s_xq[:].bitcast(BF16)

        p1 = {"norm": [Z, Z], "st": [Z, Z], "ld": [Z, Z]}

        def emit_ln_tile(src, dst, tt, src_gate=Z, h_gate=Z):
            for half in range(2):
                @on("sync")
                def _(s, half=half):
                    ctx.wait(s, p1["norm"][0])
                    ctx.wait(s, p1["norm"][1])
                    ctx.wait(s, src_gate)
                    inst = s.dma_start(
                        out=xt[:, half * 2048:(half + 1) * 2048],
                        in_=src[tt * 128:(tt + 1) * 128,
                                half * 2048:(half + 1) * 2048])
                    p1["ld"][half] = ctx.dinc(inst, "dX")

            @on("vector")
            def _(v):
                ctx.wait(v, p1["ld"][0])
                ctx.wait(v, p1["ld"][1])
                for j in range(8):
                    v.bn_stats(s_bn6[:, j * 6:(j + 1) * 6],
                               xt[:, j * 512:(j + 1) * 512])
                v.bn_aggr(s_bn2[:], s_bn6[:])
                vflush(v, "vv")
            vv_stats = ctx.mark("vv")

            @on("scalar")
            def _(s_):
                ctx.wait(s_, vv_stats)
                s_.activation(s_lnv[:], s_bn2[:, 1:2], AF.Ln, bias=c_eps[:], scale=1.0)
                s_.activation(s_rst[:], s_lnv[:], AF.Exp, scale=-0.5)
                sflush(s_, "ss")
            ss_rstd = ctx.mark("ss")

            @on("vector")
            def _(v):
                ctx.wait(v, ss_rstd)
                v.tensor_scalar(s_mb[:], s_bn2[:, 0:1], s_rst[:], -1.0,
                                ALU.mult, ALU.mult)
                vflush(v, "vv")
            vv_mb = ctx.mark("vv")

            for half in range(2):
                @on("scalar")
                def _(s_, half=half):
                    ctx.wait(s_, vv_mb)
                    ctx.wait(s_, p1["st"][half])
                    ctx.wait(s_, h_gate)
                    s_.activation(hout[:, half * 2048:(half + 1) * 2048],
                                  xt[:, half * 2048:(half + 1) * 2048],
                                  AF.Identity, bias=s_mb[:], scale=s_rst[:])
                    p1["norm"][half] = sflush(s_, "ss")

                @on("sync")
                def _(s, half=half):
                    ctx.wait(s, p1["norm"][half])
                    inst = s.dma_start(
                        out=dst[tt * 128:(tt + 1) * 128,
                                half * 2048:(half + 1) * 2048],
                        in_=hout[:, half * 2048:(half + 1) * 2048])
                    p1["st"][half] = ctx.dinc(inst, "dHW")

        def emit_ln_pass(src, dst, src_gates=None, h_gate=Z):
            for tt in range(8):
                emit_ln_tile(src, dst, tt,
                             src_gate=(src_gates[tt] if src_gates else Z),
                             h_gate=h_gate)
            return ctx.mark("dHW")

        def emit_ht_reads(src_dram, pre_gates=()):
            @on("scalar")
            def _(s_):
                for g in pre_gates:
                    ctx.wait(s_, g)
                for ft in range(32):
                    inst = s_.dma_start_transpose(
                        A[:, ft * T:(ft + 1) * T],
                        src_dram[:, ft * 128:(ft + 1) * 128])
                    ctx.dinc(inst, "dH")
                inst = s_.dma_start(out=d_fn, in_=A[:, 32 * T - 64:32 * T])
                ctx.dinc(inst, "dH")
            return ctx.mark("dH")

        PAIRS = [0, 2, 6]
        p2 = {"pair": [Z, Z, Z],
              "kslot": [Z, Z],
              "kst": Z,
              "vtrpe": [Z, Z],
              "vtrev": [Z, Z],
              "vtrdone": [Z, Z]}

        def ht_tile(ft):
            return A[:, ft * T:(ft + 1) * T]

        def emit_p2(half, own):
            korder = ([1, 0] + list(range(2, NKV))) if half == 0 else list(range(NKV))
            for idx, kc in enumerate(korder):
                pb = PAIRS[idx % 3]
                pe_stop = emit_proj_chunk(
                    wk[:, kc * 128:(kc + 1) * 128],
                    lambda ft: ht_tile(ft), pb,
                    p2["pair"][idx % 3])
                kdest = s_p2[:, 5120:6144]
                bg = max(p2["vtrev"], key=lambda m: m[1])
                vv_mark = emit_rope_evac(pb, kdest, pe_stop,
                                         extra_gate=p2["kst"],
                                         bank_gate=bg)
                p2["pair"][idx % 3] = vv_mark

                @on("sync")
                def _(s, kc=kc, kdest=kdest, vv_mark=vv_mark):
                    ctx.wait(s, vv_mark)
                    inst = s.dma_start(
                        out=d_K[kc, :, half * T:(half + 1) * T], in_=kdest)
                    p2["kst"] = ctx.dinc(inst, "dKV")


            for kc in range(NKV):
                pb = PAIRS[kc % 3]
                pe_stop = emit_proj_chunk(
                    wv[:, kc * 128:(kc + 1) * 128],
                    lambda ft: ht_tile(ft), pb,
                    p2["pair"][kc % 3])
                vdest = s_p2[:, (kc % 2) * 1024:(kc % 2 + 1) * 1024]

                @on("vector")
                def _(v, pe_stop=pe_stop, vdest=vdest, kc=kc):
                    ctx.wait(v, pe_stop)
                    ctx.wait(v, p2["vtrdone"][kc % 2])
                    v.tensor_copy(vdest, bank(pb, 2))
                    p2["pair"][kc % 3] = vflush(v, "vv")
                vv_evac = ctx.mark("vv")

                vstg = 2048 + (kc % 2) * 1024
                for st in range(8):
                    tb = 4 + (st % 2)

                    @on("tensor")
                    def _(t, st=st, tb=tb, vdest=vdest, vv_evac=vv_evac):
                        ctx.wait(t, vv_evac)
                        ctx.wait(t, p2["vtrev"][st % 2])
                        pt = bank(tb)[:, 0:64].bitcast(BF16)
                        inst = t.transpose(pt, vdest[:, st * 128:(st + 1) * 128],
                                           c_idt[:])
                        p2["vtrpe"][st % 2] = ctx.inc(inst, "pe")

                    @on("vector")
                    def _(v, st=st, tb=tb, kc=kc):
                        ctx.wait(v, p2["vtrpe"][st % 2])
                        ctx.wait(v, p2["kslot"][kc % 2])
                        v.tensor_copy(
                            s_p2[:, vstg + st * 128:vstg + (st + 1) * 128],
                            bank(tb)[:, 0:64].bitcast(BF16))
                        p2["vtrev"][st % 2] = vflush(v, "vv")
                p2["vtrdone"][kc % 2] = ctx.mark("pe")
                vstg_done = ctx.mark("vv")

                @on("sync")
                def _(s, kc=kc, vstg=vstg, vstg_done=vstg_done):
                    ctx.wait(s, vstg_done)
                    inst = s.dma_start(
                        out=d_V[kc, half * 8:(half + 1) * 8, :, :]
                        .rearrange("s p d -> p s d"),
                        in_=s_p2[:, vstg:vstg + 1024]
                        .rearrange("p (s d) -> p s d", s=8))
                    p2["kslot"][kc % 2] = ctx.dinc(inst, "dKV")

            if own:
                for qc in range(32):
                    pb = PAIRS[qc % 3]
                    pe_stop = emit_proj_chunk(
                        wq[:, qc * 128:(qc + 1) * 128],
                        lambda ft: ht_tile(ft), pb,
                        p2["pair"][qc % 3])
                    dest = QG[qc // 4][:, (qc % 4) * T:(qc % 4 + 1) * T]
                    bg = max(p2["vtrev"], key=lambda m: m[1])
                    p2["pair"][qc % 3] = emit_rope_evac(pb, dest, pe_stop,
                                                        bank_gate=bg)

        wp = {"free": [Z, Z], "ld": [Z, Z], "sc": Z, "scfree": Z,
              "st": [Z, Z], "v": [Z, Z]}

        def qv_slot(sl):
            return A[:, sl * D:(sl + 1) * D]

        def emit_wprep(oc):
            sl = oc % 2

            @on("gpsimd")
            def _(g):
                ctx.wait(g, wp["free"][sl])
                inst = g.dma_start(out=qv_slot(sl),
                                   in_=qvals[oc * 128:(oc + 1) * 128, :],
                                   max_dma_last_dim=8192)
                ctx.dinc(inst, "dQV")
                inst = g.dma_start(out=d_fn, in_=qv_slot(sl)[:, D - 64:D])
                wp["ld"][sl] = ctx.dinc(inst, "dQV")
                if oc > 0:
                    ctx.wait(g, wp["scfree"])
                inst = g.dma_start(out=A[:, 2 * D:2 * D + 256].bitcast(F32),
                                   in_=qs[oc * 128:(oc + 1) * 128, :])
                wp["sc"] = ctx.dinc(inst, "dQV")

            @on("vector")
            def _(v):
                ctx.wait(v, wp["ld"][sl])
                ctx.wait(v, wp["sc"])
                ctx.wait(v, wp["st"][sl])
                qvs = qv_slot(sl)
                v.tensor_scalar_add(qvs, qvs, -128.0)
                inst = v.tensor_tensor(
                    qvs.rearrange("p (n r) -> p n r", r=32),
                    qvs.rearrange("p (n r) -> p n r", r=32),
                    A[:, 2 * D:2 * D + 256].bitcast(F32).unsqueeze(2)
                    .broadcast_to([128, 128, 32]),
                    ALU.mult)
                wp["v"][sl] = vflush(v, "vv")
                wp["scfree"] = wp["v"][sl]

            @on("sync")
            def _(s):
                ctx.wait(s, wp["v"][sl])
                inst = s.dma_start(out=d_W[oc * 128:(oc + 1) * 128, :],
                                   in_=qv_slot(sl))
                wp["st"][sl] = ctx.dinc(inst, "dWW")
                wp["free"][sl] = wp["v"][sl]

        def probs_ap(par, m):
            base = 8448 + par * 4608
            return A[:, base + m * 512:base + (m + 1) * 512]

        def kh_ap(par):
            return s_p2[:, par * 2048:(par + 1) * 2048]

        def vh_ap(par):
            if par == 0:
                return s_p2[:, 4096:6144]
            return A[:, 17664:19712]

        p3 = {"kvld": [Z, Z], "kvfree": [Z, Z],
              "qkfree": [Z, Z], "exp": [Z] * 9, "mask": [Z, Z],
              "probsfree": [Z, Z], "avfree": [Z, Z], "dsfree": [Z, Z],
              "bcfree": [Z, Z], "avpe": [Z, Z], "recip": Z, "bcpe": [Z, Z]}

        def emit_kv_fetch(h, par, extra_gates=()):
            @on("sync")
            def _(s):
                ctx.wait(s, p3["kvfree"][par])
                for g in extra_gates:
                    ctx.wait(s, g)
                i1 = s.dma_start(out=kh_ap(par), in_=d_K[h])
                ctx.dinc(i1, "dKV")
                i2 = s.dma_start(
                    out=vh_ap(par).rearrange("p (s d) -> p s d", s=16),
                    in_=d_V[h].rearrange("s p d -> p s d"))
                p3["kvld"][par] = ctx.dinc(i2, "dKV")

        def emit_p3(kv_written, p2_vv_done, a_gate):
            p3["qkfree"] = [p2_vv_done] * 4
            for nm in ("probsfree", "avfree"):
                p3[nm] = [p2_vv_done, p2_vv_done]
            p3["dsfree"] = p2_vv_done
            p3["bcfree"] = p2_vv_done
            wp["free"] = [a_gate, a_gate]
            emit_kv_fetch(0, 0, extra_gates=[p2_vv_done, kv_written])
            for h in range(NKV):
                phase_barrier(2)
                hb = h % 2
                if h + 1 < NKV:
                    emit_kv_fetch(h + 1, (h + 1) % 2,
                                  extra_gates=[a_gate, kv_written])
                for qt in range(8):
                    it = h * 8 + qt
                    par = it % 2
                    qk_pe = [None] * 9
                    for m in range(9):
                        bq = m % 4

                        @on("tensor")
                        def _(t, m=m, bq=bq, h=h, qt=qt, par=par, hb=hb):
                            if m == 0:
                                ctx.wait(t, p3["kvld"][hb])
                                ctx.wait(t, p3["probsfree"][par])
                                ctx.wait(t, p3["avfree"][par])
                                ctx.wait(t, p3["dsfree"])
                            ctx.wait(t, p3["qkfree"][bq])
                            rhs = QG[h][:].rearrange("p (g x) -> p g x", g=G)[
                                :, :, qt * 128:(qt + 1) * 128]
                            inst = t.matmul(
                                bank(bq),
                                kh_ap(hb)[:, (qt + m) * 128:(qt + m + 1) * 128],
                                rhs, start=True, stop=True)
                            qk_pe[m] = ctx.inc(inst, "pe")

                        @on("scalar")
                        def _(s_, m=m, bq=bq, qt=qt, par=par):
                            ctx.wait(s_, qk_pe[m])
                            if m < 2:
                                ctx.wait(s_, p3["probsfree"][par])
                            st = qt + m
                            inst = s_.activation(
                                probs_ap(par, m), bank(bq), AF.Exp,
                                bias=c_sb[:, st:st + 1], scale=SM)
                            p3["exp"][m] = ctx.inc(inst, "ss")
                            p3["qkfree"][bq] = p3["exp"][m]

                        if m in (0, 8):
                            @on("vector")
                            def _(v, m=m, par=par):
                                ctx.wait(v, p3["exp"][m])
                                msk = c_m0 if m == 0 else c_m8
                                pa = probs_ap(par, m).rearrange(
                                    "p (g x) -> p g x", g=G)
                                v.tensor_tensor(
                                    pa, pa,
                                    msk[:].unsqueeze(1)
                                    .broadcast_to([128, G, 128]),
                                    ALU.mult)
                                p3["mask"][m == 8] = vflush(v, "vv")

                    @on("tensor")
                    def _(t, h=h, qt=qt, par=par, hb=hb):
                        for m in range(9):
                            if m in (0, 8):
                                ctx.wait(t, p3["mask"][m == 8])
                            else:
                                ctx.wait(t, p3["exp"][m])
                            st = qt + m
                            t.matmul(bank(4 + par),
                                     vh_ap(hb)[:, st * 128:(st + 1) * 128],
                                     probs_ap(par, m),
                                     start=(m == 0), stop=(m == 8))
                            inst = t.matmul(
                                p_all[0:1, 6 * 512:7 * 512],
                                c_one[:], probs_ap(par, m),
                                start=(m == 0), stop=(m == 8))
                        p3["avpe"][par] = ctx.inc(inst, "pe")
                        p3["probsfree"][par] = p3["avpe"][par]
                        if qt == 7:
                            p3["kvfree"][hb] = p3["avpe"][par]

                    @on("vector")
                    def _(v, par=par):
                        ctx.wait(v, p3["avpe"][par])
                        v.reciprocal(s_dsf[:],
                                     p_all[0:1, 6 * 512:7 * 512])
                        v.tensor_copy(s_dsb[:, 0:512], s_dsf[:])
                        v.tensor_tensor(s_dsf[:], s_dsf[:], s_dsb[:, 0:512],
                                        ALU.subtract)
                        v.tensor_copy(s_dsb[:, 512:1024], s_dsf[:])
                        p3["dsfree"] = vflush(v, "vv")
                        p3["recip"] = p3["dsfree"]

                    @on("tensor")
                    def _(t, par=par):
                        ctx.wait(t, p3["recip"])
                        ctx.wait(t, p3["bcfree"])
                        t.matmul(bank(7), c_ok1[:], s_dsb[:, 0:512],
                                 start=True, stop=False)
                        inst = t.matmul(bank(7), c_ok1[:],
                                        s_dsb[:, 512:1024],
                                        start=False, stop=True)
                        p3["bcpe"][par] = ctx.inc(inst, "pe")

                    @on("vector")
                    def _(v, h=h, qt=qt, par=par):
                        ctx.wait(v, p3["bcpe"][par])
                        v.tensor_copy(s_bcb[:], bank(7))
                        p3["bcfree"] = vflush(v, "vv")
                        dest = QG[h][:].rearrange("p (g x) -> p g x", g=G)[
                            :, :, qt * 128:(qt + 1) * 128]
                        inst = v.tensor_tensor(
                            dest, bank(4 + par),
                            s_bcb[:].rearrange("p (g x) -> p g x", g=G),
                            ALU.mult)
                        p3["avfree"][par] = ctx.inc(inst, "vv")

                    if it % 2 == 1 and (it - 1) // 2 < 32:
                        emit_wprep((it - 1) // 2)

        bp = {"pair": [Z, Z], "pe": [Z, Z], "evst": [Z, Z]}

        def emit_bigproj(lhs_col_fn, rhs_fn, dst_dram, bias_fn, st_sem,
                         cast, queue_name, shuttle_cb=None, halfgrp_marks=None):
            for fc in range(32):
                pb = (fc % 2) * 2
                pe_stop = emit_proj_chunk(lhs_col_fn(fc), rhs_fn, pb,
                                          bp["pair"][fc % 2], cast=cast,
                                          queue_name=queue_name)
                bp["pe"][fc % 2] = pe_stop

                @on("scalar")
                def _(s_, fc=fc, pb=pb):
                    ctx.wait(s_, bp["pe"][fc % 2])
                    ctx.wait(s_, bp["evst"][fc % 2])
                    eslot = s_p2[:, (fc % 2) * 1024:(fc % 2 + 1) * 1024]
                    if bias_fn is not None:
                        s_.activation(eslot, bank(pb, 2), AF.Identity,
                                      bias=bias_fn(fc), scale=1.0)
                    else:
                        s_.activation(eslot, bank(pb, 2), AF.Identity,
                                      scale=1.0)
                    bp["pair"][fc % 2] = sflush(s_, "ss")

                @on("sync")
                def _(s, fc=fc):
                    ctx.wait(s, bp["pair"][fc % 2])
                    eslot = s_p2[:, (fc % 2) * 1024:(fc % 2 + 1) * 1024]
                    inst = s.dma_start(
                        out=dst_dram[fc * 128:(fc + 1) * 128, :], in_=eslot)
                    bp["evst"][fc % 2] = ctx.dinc(inst, st_sem)
                if halfgrp_marks is not None and fc in (15, 31):
                    halfgrp_marks[fc // 16] = ctx.mark(st_sem)
                if shuttle_cb is not None:
                    shuttle_cb(fc)

        pl = {"xq": [Z, Z], "xld": [Z, Z], "yst": [Z, Z], "rd": [Z, Z, Z, Z]}

        def emit_postln(src_dram, x_src, dst, halfgrp_marks, x_gate, out_sem,
                        after_tile=None):
            for tt in range(8):
                sl2 = (tt % 2) * 2

                @on("scalar")
                def _(s_, tt=tt, sl2=sl2):
                    for grp in range(2):
                        ctx.wait(s_, halfgrp_marks[grp])
                        ctx.wait(s_, pl["rd"][sl2 + grp])
                        inst = s_.dma_start_transpose(
                            hx(sl2 + grp),
                            src_dram[grp * 2048:(grp + 1) * 2048,
                                     tt * 128:(tt + 1) * 128])
                        ctx.dinc(inst, "dAT")
                    inst = s_.dma_start(out=d_fn,
                                        in_=hx(sl2 + 1)[:, 2048 - 64:2048])
                    ctx.dinc(inst, "dAT")
                    s_.wait_ge(ctx.sem["dAT"], ctx.c["dAT"])
                rd_done = ctx.mark("dAT")

                @on("vector")
                def _(v, tt=tt):
                    ctx.wait(v, rd_done)
                    for j in range(8):
                        v.bn_stats(s_bn6[:, j * 6:(j + 1) * 6],
                                   hx(sl2 + j // 4)[:, (j % 4) * 512:
                                                    (j % 4 + 1) * 512])
                    v.bn_aggr(s_bn2[:], s_bn6[:])
                    vflush(v, "vv")
                vv_stats = ctx.mark("vv")

                @on("scalar")
                def _(s_, tt=tt):
                    ctx.wait(s_, vv_stats)
                    s_.activation(s_lnv[:], s_bn2[:, 1:2], AF.Ln, bias=c_eps[:],
                                  scale=1.0)
                    s_.activation(s_rst[:], s_lnv[:], AF.Exp, scale=-0.5)
                    sflush(s_, "ss")
                ss_rstd = ctx.mark("ss")

                @on("vector")
                def _(v, tt=tt):
                    ctx.wait(v, ss_rstd)
                    v.tensor_scalar(s_mb[:], s_bn2[:, 0:1], s_rst[:], 1.0,
                                    ALU.mult, ALU.mult)
                    vflush(v, "vv")
                vv_mr = ctx.mark("vv")
                quarter_vv = [None] * 4

                for q in range(4):
                    sl = q % 2

                    @on("sync")
                    def _(s, tt=tt, q=q, sl=sl):
                        ctx.wait(s, pl["xq"][sl])
                        ctx.wait(s, x_gate)
                        inst = s.dma_start(
                            out=s_xq[:, sl * T:(sl + 1) * T],
                            in_=x_src[tt * 128:(tt + 1) * 128,
                                      q * T:(q + 1) * T])
                        pl["xld"][sl] = ctx.dinc(inst, "dX1")

                    @on("vector")
                    def _(v, tt=tt, q=q, sl=sl):
                        ctx.wait(v, vv_mr)
                        ctx.wait(v, pl["xld"][sl])
                        ctx.wait(v, pl["yst"][sl])
                        src = hx(sl2 + q // 2)[:, (q % 2) * T:(q % 2 + 1) * T]
                        v.tensor_scalar(s_y[:, sl * T:(sl + 1) * T], src,
                                        s_rst[:], s_mb[:], ALU.mult,
                                        ALU.subtract)
                        v.tensor_tensor(s_y[:, sl * T:(sl + 1) * T],
                                        s_y[:, sl * T:(sl + 1) * T],
                                        s_xq[:, sl * T:(sl + 1) * T],
                                        ALU.add)
                        pl["xq"][sl] = vflush(v, "vv")
                        quarter_vv[q] = pl["xq"][sl]

                    @on("sync")
                    def _(s, tt=tt, q=q, sl=sl):
                        ctx.wait(s, pl["xq"][sl])
                        inst = s.dma_start(
                            out=dst[tt * 128:(tt + 1) * 128, q * T:(q + 1) * T],
                            in_=s_y[:, sl * T:(sl + 1) * T])
                        pl["yst"][sl] = ctx.dinc(inst, out_sem)

                pl["rd"][sl2] = quarter_vv[1]
                pl["rd"][sl2 + 1] = quarter_vv[3]
                if after_tile is not None:
                    after_tile(tt)

        sh = {"free": [Z, Z], "rd": 0}

        def emit_shuttle_step(it, dww_all):
            sl = it % 2

            @on("sync")
            def _(s):
                ctx.wait(s, dww_all)
                ctx.wait(s, sh["free"][sl])
                inst = s.dma_start_transpose(
                    qv_slot(sl), d_W[:, it * 128:(it + 1) * 128])
                ctx.dinc(inst, "dWT")
                inst = s.dma_start(out=d_fn, in_=qv_slot(sl)[:, D - 64:D])
                ctx.dinc(inst, "dWT")
                s.wait_ge(ctx.sem["dWT"], ctx.c["dWT"])
                inst = s.dma_start(out=d_Wt[it * 128:(it + 1) * 128, :],
                                   in_=qv_slot(sl))
                sh["free"][sl] = ctx.dinc(inst, "dWX")


        @on("gpsimd")
        def _(g):
            ctx.wait(g, consts_done)
        emit_ln_pass(xh, d_h[0])
        phase_barrier()
        hA_done = ctx.mark("dHW")
        ht_mark = emit_ht_reads(d_h[0], pre_gates=[hA_done])
        emit_ln_pass(xo, d_h[1])
        phase_barrier()
        hB_done = ctx.mark("dHW")

        @on("tensor")
        def _(t):
            ctx.wait(t, ht_mark)
            ctx.wait(t, consts_done)
        emit_p2(0, own=False)
        phase_barrier()
        p2A_vv = ctx.mark("vv")

        @on("sync")
        def _(s):
            ctx.wait(s, p2A_vv)
            ctx.dinc(s.dma_start(out=c_cs1[:], in_=cs1o), "dC")
            ctx.dinc(s.dma_start(out=c_cs2[:], in_=cs2o), "dC")
        cs_own = ctx.mark("dC")

        p2A_pe = ctx.mark("pe")
        ht_mark2 = emit_ht_reads(d_h[1], pre_gates=[hB_done, p2A_pe])

        @on("tensor")
        def _(t):
            ctx.wait(t, ht_mark2)

        @on("vector")
        def _(v):
            ctx.wait(v, cs_own)
        emit_p2(1, own=True)
        phase_barrier()
        kv_written = ctx.mark("dKV")
        p2B_vv = ctx.mark("vv")
        p2B_pe = ctx.mark("pe")

        emit_p3(kv_written, p2B_vv, p2B_pe)
        phase_barrier()
        p3_done_vv = ctx.mark("vv")
        p3_done_pe = ctx.mark("pe")
        dww_all = ctx.mark("dWW")

        halfgrp = [None, None]
        bp["pair"] = [p3_done_vv, p3_done_vv]
        bp["evst"] = [p3_done_pe, p3_done_pe]

        @on("tensor")
        def _(t):
            ctx.wait(t, p3_done_vv)
        emit_bigproj(
            lhs_col_fn=lambda fc: wo[:, fc * 128:(fc + 1) * 128],
            rhs_fn=lambda hdt: QG[hdt // 4][:, (hdt % 4) * T:(hdt % 4 + 1) * T],
            dst_dram=d_ar, bias_fn=None, st_sem="dAR", cast=True,
            queue_name="gpsimd",
            halfgrp_marks=halfgrp)
        phase_barrier()
        pl["rd"] = [p1["norm"][0], p1["norm"][1],
                    p1["norm"][0], p1["norm"][1]]
        emit_postln(d_ar, xo, d_x1, halfgrp, Z, out_sem="dX1",
                    after_tile=lambda tt: [emit_shuttle_step(4 * tt + j, dww_all)
                                           for j in range(4)])
        phase_barrier()
        x1_done = ctx.mark("dX1")
        shuttle_done = ctx.mark("dWX")
        p1["norm"] = [max(pl["rd"][0], pl["rd"][1], key=lambda m: m[1]),
                      max(pl["rd"][2], pl["rd"][3], key=lambda m: m[1])]

        emit_ln_pass(d_x1, d_h2, src_gates=[x1_done] * 8,
                     h_gate=ctx.mark("vv"))
        phase_barrier()
        h2_done = ctx.mark("dHW")
        p4_pe = ctx.mark("pe")
        ht_mark3 = emit_ht_reads(d_h2, pre_gates=[h2_done, shuttle_done, p4_pe])

        @on("tensor")
        def _(t):
            ctx.wait(t, ht_mark3)

        @on("sync")
        def _(s):
            ctx.wait(s, shuttle_done)

        halfgrp2 = [None, None]
        emit_bigproj(
            lhs_col_fn=lambda oc: d_Wt[:, oc * 128:(oc + 1) * 128],
            rhs_fn=lambda it: ht_tile(it),
            dst_dram=d_fr, bias_fn=lambda oc: c_br[:, oc:oc + 1],
            st_sem="dAR", cast=False, queue_name="sync",
            halfgrp_marks=halfgrp2)
        phase_barrier()

        pl["rd"] = [p1["norm"][0], p1["norm"][1],
                    p1["norm"][0], p1["norm"][1]]
        emit_postln(d_fr, d_x1, y, halfgrp2, x1_done, out_sem="dY")

        for tname, (tap_ap, src_ap) in tap_out.items():
            @on("sync")
            def _(s, tap_ap=tap_ap, src_ap=src_ap):
                s.wait_ge(ctx.sem["dY"], ctx.c["dY"])
                flat_src = src_ap
                inst = s.dma_start(out=tap_ap, in_=flat_src)
                ctx.dinc(inst, "dTAP")

        @on("sync")
        def _(s):
            s.wait_ge(ctx.sem["dY"], ctx.c["dY"])
            if taps:
                s.wait_ge(ctx.sem["dTAP"], ctx.c["dTAP"])

    return nc


def _host_consts(core):
    b, k = core // 4, core % 4
    f = np.arange(64, dtype=np.float64)
    inv = 10000.0 ** (-f / 64.0)

    def cs_tables(pos0):
        pos = np.arange(pos0, pos0 + T, dtype=np.float64)
        ang = pos[None, :] * inv[:, None]
        c = np.cos(ang)
        s = np.sin(ang)
        cc = np.concatenate([c, c], 0).astype(np.float32)
        ss = np.concatenate([-s, s], 0).astype(np.float32)
        return cc, ss

    cs1o, cs2o = cs_tables(k * T)
    if k > 0:
        cs1h, cs2h = cs_tables((k - 1) * T)
    else:
        cs1h, cs2h = cs1o, cs2o

    sp = np.arange(128)[:, None]
    qp = np.arange(128)[None, :]
    m0 = (sp > qp).astype(BF)
    m8 = (sp <= qp).astype(BF)

    sb = np.full((128, 16), ESH, dtype=np.float32)
    if k == 0:
        sb[:, 0:8] = NEGB

    one = np.ones((128, 1), dtype=BF)
    ok1 = np.ones((1, 128), dtype=BF)
    idt = np.eye(128, dtype=BF)
    prm = np.zeros((128, 128), dtype=BF)
    prm[(np.arange(128) + 64) % 128, np.arange(128)] = 1
    eps = np.full((128, 1), EPS, dtype=np.float32)
    return dict(cs1o=cs1o, cs2o=cs2o, cs1h=cs1h, cs2h=cs2h, m0i=m0, m8i=m8,
                sbi=sb, onei=one, onk1=ok1, idti=idt, epsi=eps, prmi=prm)


_NC_CACHE = {}


def _get_nc(taps=()):
    key = tuple(taps)
    if key not in _NC_CACHE:
        _NC_CACHE[key] = build(taps)
    return _NC_CACHE[key]


def make_in_maps(L0_attn_x, L0_attn_wq, L0_attn_wk, L0_attn_wv, L0_attn_wo,
                 L0_ffn_q_scales, L0_ffn_q_vals, L0_ffn_bias):
    x = np.ascontiguousarray(np.asarray(L0_attn_x, dtype=np.float32))
    wq = np.ascontiguousarray(np.asarray(L0_attn_wq, dtype=np.float32))
    wk = np.ascontiguousarray(np.asarray(L0_attn_wk, dtype=np.float32))
    wv = np.ascontiguousarray(np.asarray(L0_attn_wv, dtype=np.float32))
    wo = np.ascontiguousarray(np.asarray(L0_attn_wo, dtype=np.float32))
    qs = np.ascontiguousarray(np.asarray(L0_ffn_q_scales, dtype=np.float32))
    qvv = np.ascontiguousarray(
        np.asarray(L0_ffn_q_vals, dtype=np.int32).reshape(D, D))
    bias = np.asarray(L0_ffn_bias, dtype=np.float32)
    br = np.ascontiguousarray(bias.reshape(32, 128).T)

    in_maps = []
    for c in range(8):
        b, k = c // 4, c % 4
        xo = np.ascontiguousarray(x[b, k * T:(k + 1) * T, :])
        xhs = x[b, (k - 1) * T:k * T, :] if k > 0 else x[b, 0:T, :]
        xh = np.ascontiguousarray(xhs)
        m = dict(xo=xo, xh=xh, wq=wq, wk=wk, wv=wv, wo=wo, qs=qs, qvals=qvv,
                 bri=br, **_host_consts(c))
        in_maps.append(m)
    return in_maps


def kernel(L0_attn_x, L0_attn_wq, L0_attn_wk, L0_attn_wv, L0_attn_wo,
           L0_ffn_q_scales, L0_ffn_q_vals, L0_ffn_bias):
    in_maps = make_in_maps(L0_attn_x, L0_attn_wq, L0_attn_wk, L0_attn_wv,
                           L0_attn_wo, L0_ffn_q_scales, L0_ffn_q_vals,
                           L0_ffn_bias)
    nc = _get_nc()
    res = run_bass_kernel_spmd(nc, in_maps, core_ids=list(range(8)))
    out = np.empty((2, 4096, D), dtype=np.float32)
    for c in range(8):
        b, k = c // 4, c % 4
        out[b, k * T:(k + 1) * T, :] = res.results[c]["y"]
    return out
